# version 18 (final)
# speedup vs baseline: 1.0340x; 1.0134x over previous
import numpy as np
import ml_dtypes
import concourse.bass as bass
import concourse.mybir as mybir
from concourse.ap import AP

F32 = mybir.dt.float32
BF16 = mybir.dt.bfloat16
U32 = mybir.dt.uint32
AX = mybir.AxisListType
ALU = mybir.AluOpType
ACT = mybir.ActivationFunctionType

NCORES = 8
TOK = 2048
EXT = TOK + 128
NT = EXT // 128
DM = 1024
H = 16
DH = 64
CH = 64
NCH_TOT = 256
E = 8
FF = 2048

TT_EXT = [(0, 512), (512, 512), (1024, 512), (1536, 512), (2048, 128)]
TT_TOK = [(0, 512), (512, 512), (1024, 512), (1536, 512)]




_SBCNT = [0]


def _mk_sb(nc, C, fx):
    def SB(shape, dt):
        if fx is not None:
            return fx['alloc'](shape, dt)
        _SBCNT[0] += 1
        return C(nc.sbuf_tensor(f"sb{_SBCNT[0]}", shape, dt))
    return SB


def _mk_ps(nc, C, fx):
    def PS(idx, shape):
        if fx is not None:
            b = fx['banks'][idx]
            v = b[0:shape[0], 0:shape[-1]]
            return v
        _SBCNT[0] += 1
        return C(nc.psum_tensor(f"ps{_SBCNT[0]}", shape, F32))
    return PS


def mk_nc():
    return bass.Bass("TRN2", target_bir_lowering=False, debug=False, num_devices=8)


def chunk_consts():
    Cf = float(CH)
    w = np.zeros((NCH_TOT, CH), np.float32)
    a = np.zeros((NCH_TOT, CH), np.float32)
    c = np.zeros((NCH_TOT, CH), np.float32)
    c1 = np.zeros((NCH_TOT,), np.float32)
    c2 = np.zeros((NCH_TOT,), np.float32)
    Tm = np.zeros((NCH_TOT, CH, CH), np.float32)
    for i in range(NCH_TOT):
        ts = float(i * CH)
        idx = np.arange(1, CH + 1, dtype=np.float64) + ts
        cs = np.cumsum(idx)
        weighted_idx = idx + idx.sum() - cs
        w[i] = weighted_idx / ((1 + ts + Cf) * (ts + Cf) / 2.0)
        c1[i] = ts / (ts + Cf) * ((1 + ts) / (1 + ts + Cf))
        c2[i] = Cf / (ts + Cf) * ((ts * 2 + 1 + Cf) / (1 + ts + Cf))
        tri0 = (1 + ts) * ts / 2.0
        a[i] = tri0 / (tri0 + cs)
        c[i] = 1.0 - a[i]
        r = cs[:, None]
        j_idx = (cs - idx)[None, :]
        Tm[i] = np.where(r >= j_idx, (r - j_idx) / (tri0 + r), 0.0)
    return dict(w=w, a=a, c=c, coef1=c1, coef2=c2, Tmat=Tm)


def ind16_consts():
    ind = np.zeros((8, 128, 16), np.float32)
    for fc in range(8):
        for fp in range(128):
            ind[fc, fp, 2 * fc + (fp // 64)] = 1.0
    indb = ind.astype(ml_dtypes.bfloat16)
    indTb = np.ascontiguousarray(ind.transpose(0, 2, 1)).astype(ml_dtypes.bfloat16)
    return indb, indTb


def hostprep_A(inp, core):
    x = np.asarray(inp['L0_attn_x'])[0]
    t0 = core * TOK
    ext = np.zeros((EXT, DM), np.float32)
    lo = max(t0 - 128, 0)
    ext[128 - (t0 - lo):128] = x[lo:t0]
    ext[128:] = x[t0:t0 + TOK]
    indb, indTb = ind16_consts()
    m = {
        'x_ext': ext,
        'halo_mask': np.full((128, 1), 0.0 if core == 0 else 1.0, np.float32),
        'Wk': np.asarray(inp['L0_attn_Wk']),
        'Wq': np.asarray(inp['L0_attn_Wq']),
        'Wv': np.asarray(inp['L0_attn_Wv']),
        'Wbeta': np.asarray(inp['L0_attn_Wbeta']),
        'bk_t': np.asarray(inp['L0_attn_bk']).reshape(8, 128).T.copy(),
        'bq_t': np.asarray(inp['L0_attn_bq']).reshape(8, 128).T.copy(),
        'bv_t': np.asarray(inp['L0_attn_bv']).reshape(8, 128).T.copy(),
        'bbeta_t': np.asarray(inp['L0_attn_bbeta']).reshape(16, 1).copy(),
        'ck': np.asarray(inp['L0_attn_conv_k']).reshape(8, 128, 4).transpose(1, 0, 2).copy(),
        'cq': np.asarray(inp['L0_attn_conv_q']).reshape(8, 128, 4).transpose(1, 0, 2).copy(),
        'cv': np.asarray(inp['L0_attn_conv_v']).reshape(8, 128, 4).transpose(1, 0, 2).copy(),
        'ind16': indb,
        'ind16T': indTb,
    }
    return m


def build_phaseA(nc, fx=None):
    d = {}
    if fx is not None:
        d = fx['d']
        outs = fx['outsA']
        wbf = fx['wbf']
    if fx is None:
        d['x_ext'] = nc.dram_tensor("x_ext", [EXT, DM], F32, kind="ExternalInput").ap()
        d['halo_mask'] = nc.dram_tensor("halo_mask", [128, 1], F32, kind="ExternalInput").ap()
        for nm in ("Wk", "Wq", "Wv"):
            d[nm] = nc.dram_tensor(nm, [DM, DM], F32, kind="ExternalInput").ap()
        d['Wbeta'] = nc.dram_tensor("Wbeta", [H, DM], F32, kind="ExternalInput").ap()
        for nm in ("bk_t", "bq_t", "bv_t"):
            d[nm] = nc.dram_tensor(nm, [128, 8], F32, kind="ExternalInput").ap()
        d['bbeta_t'] = nc.dram_tensor("bbeta_t", [16, 1], F32, kind="ExternalInput").ap()
        for nm in ("ck", "cq", "cv"):
            d[nm] = nc.dram_tensor(nm, [128, 8, 4], F32, kind="ExternalInput").ap()
        d['ind16'] = nc.dram_tensor("ind16", [8, 128, 16], BF16, kind="ExternalInput").ap()
        d['ind16T'] = nc.dram_tensor("ind16T", [8, 16, 128], BF16, kind="ExternalInput").ap()
        outs = {}
        for nm in ("kh_o", "bt_o", "qh_o", "vt_o", "xn_o"):
            outs[nm] = nc.dram_tensor(nm, [128, 8, TOK], BF16, kind="ExternalOutput").ap()
        wbf = nc.dram_tensor("wbf", [DM, DM], BF16, kind="Internal").ap()

    sched = []
    for k in range(len(TT_EXT)):
        sched.append(('beta', k))
    for tau, nm in enumerate(('k', 'q', 'v')):
        for jc in range(8):
            for k in range(len(TT_EXT)):
                sched.append(('proj', tau, jc, k))
        if tau < 2:
            for k4 in range(len(TT_TOK)):
                sched.append(('norm', tau, k4))
        nslots = 2 if tau == 0 else 1
        for slot in range(nslots):
            for fc in range(8):
                for k4 in range(len(TT_TOK)):
                    sched.append(('bcast', tau, slot, fc, k4))
    NMM = len(sched)

    import contextlib
    es = contextlib.ExitStack()
    C = es.enter_context
    SB = _mk_sb(nc, C, fx)
    PS = _mk_ps(nc, C, fx)
    big0 = SB([128, NT * DM], BF16)
    xn_tm = big0[:].rearrange("p (a b) -> p a b", a=NT)
    raw = big0[:].rearrange("p (a b) -> p a b", a=8)
    xn_fm = SB([128, 8, EXT], BF16)
    x_db = SB([128, 2, DM], F32)
    stats = SB([128, 8], F32)
    f32scr = SB([128, TOK], F32)
    sq_scr = f32scr[:, 0:DM]
    wT = SB([128, 2, 8, DM], BF16)
    wbt_nat = SB([16, DM], BF16)
    wbt = SB([128, 8, 16], BF16)
    biases = SB([128, 3, 8], F32)
    bbeta_sb = SB([16, 1], F32)
    convw = SB([128, 3, 8, 4], F32)
    ind16_sb = SB([128, 8, 16], BF16)
    ind16T_sb = SB([16, 8, 128], BF16)
    hmask = SB([128, 1], F32)
    eta_sb = SB([16, EXT], BF16)
    n2_sb = SB([16, TOK], F32)
    scal_sb = SB([16, 2, TOK], BF16)
    conv_o = SB([128, 8, TOK], BF16)
    outb = SB([128, 8, TOK], BF16)
    psA = PS(0, [128, 512])
    psB = PS(1, [128, 512])
    psN = PS(2, [16, 512])
    s_xld0 = C(nc.semaphore("A_s_xld0"))
    s_xld1 = C(nc.semaphore("A_s_xld1"))
    s_wld = C(nc.semaphore("A_s_wld"))
    s_trw = C(nc.semaphore("A_s_trw"))
    s_trx = C(nc.semaphore("A_s_trx"))
    s_lnv = C(nc.semaphore("A_s_lnv"))
    s_lna = C(nc.semaphore("A_s_lna"))
    s_mm = C(nc.semaphore("A_s_mm"))
    s_ev = C(nc.semaphore("A_s_ev"))
    s_ac = C(nc.semaphore("A_s_ac"))
    s_vst = C(nc.semaphore("A_s_vst"))
    s_out = C(nc.semaphore("A_s_out"))
    if fx is not None:
        fx['s_outA'] = s_out
    block = fx['block'] if fx is not None else C(nc.Block())
    if True:
        evi = []
        n = 0
        for it in sched:
            if it[0] != 'beta':
                n += 1
            evi.append(n)

        def bank(j):
            it = sched[j]
            if it[0] in ('beta', 'norm'):
                return None
            return (psA, psB)[j % 2]

        prev_same_bank = {}
        last = {0: None, 1: None}
        for j, it in enumerate(sched):
            if it[0] in ('beta', 'norm'):
                continue
            prev_same_bank[j] = last[j % 2]
            last[j % 2] = j


        @block.gpsimd
        def _(g):
            g.dma_start(out=hmask[:], in_=d['halo_mask']).then_inc(s_wld, 16)
            g.dma_start(out=biases[:, 0, :], in_=d['bk_t']).then_inc(s_wld, 16)
            g.dma_start(out=biases[:, 1, :], in_=d['bq_t']).then_inc(s_wld, 16)
            g.dma_start(out=biases[:, 2, :], in_=d['bv_t']).then_inc(s_wld, 16)
            g.dma_start(out=bbeta_sb[:], in_=d['bbeta_t']).then_inc(s_wld, 16)
            g.dma_start(out=convw[:, 0], in_=d['ck']).then_inc(s_wld, 16)
            g.dma_start(out=convw[:, 1], in_=d['cq']).then_inc(s_wld, 16)
            g.dma_start(out=convw[:, 2], in_=d['cv']).then_inc(s_wld, 16)
            g.dma_start(out=ind16_sb[:], in_=d['ind16'].transpose([1, 0, 2])).then_inc(s_wld, 16)
            g.dma_start(out=ind16T_sb[:], in_=d['ind16T'].transpose([1, 0, 2])).then_inc(s_wld, 16)
            g.dma_start(out=wbt_nat[:], in_=d['Wbeta']).then_inc(s_wld, 16)
            g.dma_start(out=wbf[:], in_=d['Wk']).then_inc(s_wld, 16)
            for i in range(NT):
                if i >= 2:
                    g.wait_ge(s_lna, 3 * (i - 2) + 3)
                g.dma_start(out=x_db[:, i % 2, :],
                            in_=d['x_ext'][128 * i:128 * (i + 1), :]
                            ).then_inc((s_xld0, s_xld1)[i % 2], 16)
            g.wait_ge(s_trw, 72 * 16)
            g.dma_start(out=wbf[:], in_=d['Wq']).then_inc(s_wld, 16)
            g.wait_ge(s_trw, 136 * 16)
            g.dma_start(out=wbf[:], in_=d['Wv']).then_inc(s_wld, 16)
            g.wait_ge(s_vst, 5)
            g.dma_start(out=outs['kh_o'], in_=outb[:]).then_inc(s_out, 16)
            if fx is not None and 'cc_after' in fx:
                fx['cc_after'](g, 0)
            g.wait_ge(s_vst, 6)
            g.dma_start(out=outs['bt_o'], in_=raw[:, :, 0:TOK]).then_inc(s_out, 16)
            if fx is not None and 'cc_after' in fx:
                fx['cc_after'](g, 1)
            g.wait_ge(s_vst, 10)
            g.dma_start(out=outs['qh_o'], in_=outb[:]).then_inc(s_out, 16)
            if fx is not None and 'cc_after' in fx:
                fx['cc_after'](g, 2)
            g.wait_ge(s_vst, 13)
            g.dma_start(out=outs['vt_o'], in_=outb[:]).then_inc(s_out, 16)
            if fx is not None and 'cc_after' in fx:
                fx['cc_after'](g, 3)
            g.wait_ge(s_trx, 136 * 16)
            g.dma_start(out=outs['xn_o'], in_=xn_fm[:, :, 128:EXT]).then_inc(s_out, 16)
            if fx is None:
                g.wait_ge(s_out, 80)

        @block.sync
        def _(s):
            s.wait_ge(s_wld, 12 * 16)
            for jc in range(8):
                for dc in range(8):
                    s.dma_start(out=wT[:, 0, jc, dc * 128:(dc + 1) * 128],
                                in_=wbf[jc * 128:(jc + 1) * 128, dc * 128:(dc + 1) * 128],
                                transpose=True).then_inc(s_trw, 16)
            s.wait_ge(s_wld, 12 * 16)
            for dc in range(8):
                s.dma_start(out=wbt[:, dc, :], in_=wbt_nat[:, dc * 128:(dc + 1) * 128],
                            transpose=True).then_inc(s_trw, 16)
            s.wait_ge(s_wld, 13 * 16)
            for jc in range(8):
                for dc in range(8):
                    s.dma_start(out=wT[:, 1, jc, dc * 128:(dc + 1) * 128],
                                in_=wbf[jc * 128:(jc + 1) * 128, dc * 128:(dc + 1) * 128],
                                transpose=True).then_inc(s_trw, 16)
            k_done_mm = 5 + 8 * len(TT_EXT)
            s.wait_ge(s_mm, k_done_mm)
            s.wait_ge(s_wld, 14 * 16)
            for jc in range(8):
                for dc in range(8):
                    s.dma_start(out=wT[:, 0, jc, dc * 128:(dc + 1) * 128],
                                in_=wbf[jc * 128:(jc + 1) * 128, dc * 128:(dc + 1) * 128],
                                transpose=True).then_inc(s_trw, 16)

        @block.scalar
        def _(a):
            for i in range(NT):
                a.wait_ge((s_xld0, s_xld1)[i % 2], (i // 2 + 1) * 16)
                a.activation(out=sq_scr, in_=x_db[:, i % 2, :], func=ACT.Square,
                             accum_out=stats[:, 1:2]).then_inc(s_lna, 1)
                a.wait_ge(s_lnv, 2 * i + 1)
                a.drain()
                a.activation(out=stats[:, 3:4], in_=stats[:, 7:8],
                             func=ACT.Sqrt).then_inc(s_lna, 1)
                a.wait_ge(s_lnv, 2 * i + 2)
                a.drain()
                a.activation(out=xn_tm[:, i, :], in_=x_db[:, i % 2, :], func=ACT.Identity,
                             bias=stats[:, 4:5], scale=stats[:, 3:4])
                a.drain()
                for dc in range(8):
                    a.dma_start(out=xn_fm[:, dc, i * 128:(i + 1) * 128],
                                in_=xn_tm[:, i, dc * 128:(dc + 1) * 128],
                                transpose=True).then_inc(s_trx, 16)
                a.nop().then_inc(s_lna, 1)
            for k, (t0, tw) in enumerate(TT_EXT):
                a.wait_ge(s_mm, k + 1)
                a.activation(out=eta_sb[:, t0:t0 + tw], in_=psN[:, 0:tw], func=ACT.Sigmoid,
                             bias=bbeta_sb[:, 0:1]).then_inc(s_ac, 1)
            a.wait_ge(s_vst, 1)
            a.activation(out=raw[:, :, 0:TOK], in_=conv_o[:], func=ACT.Square
                         ).then_inc(s_ac, 1)
            a.wait_ge(s_vst, 2)
            a.activation(out=f32scr[0:16, :], in_=n2_sb[:], func=ACT.Sqrt
                         ).then_inc(s_ac, 1)
            a.wait_ge(s_vst, 7)
            a.activation(out=raw[:, :, 0:TOK], in_=conv_o[:], func=ACT.Square
                         ).then_inc(s_ac, 1)
            a.wait_ge(s_vst, 8)
            a.wait_ge(s_mm, 5 + 8 * len(TT_EXT) + len(TT_TOK) + 2 * 8 * len(TT_TOK))
            a.activation(out=f32scr[0:16, :], in_=n2_sb[:], func=ACT.Sqrt
                         ).then_inc(s_ac, 1)
            a.wait_ge(s_vst, 11)
            a.activation(out=conv_o[:], in_=conv_o[:], func=ACT.Silu
                         ).then_inc(s_ac, 1)

        @block.tensor
        def _(t):
            t.wait_ge(s_trx, 136 * 16)
            t.wait_ge(s_trw, 72 * 16)
            for j, it in enumerate(sched):
                if it[0] == 'beta':
                    k = it[1]
                    t0, tw = TT_EXT[k]
                    if k > 0:
                        t.wait_ge(s_ac, k)
                    for dc in range(8):
                        mi = t.matmul(psN[:, 0:tw], wbt[:, dc, :], xn_fm[:, dc, t0:t0 + tw],
                                      start=(dc == 0), stop=(dc == 7))
                    mi.then_inc(s_mm, 1)
                elif it[0] == 'proj':
                    tau, jc, k = it[1], it[2], it[3]
                    t0, tw = TT_EXT[k]
                    buf = (0, 1, 0)[tau]
                    if tau == 0 and jc == 0 and k == 0:
                        pass
                    if tau == 1 and jc == 0 and k == 0:
                        t.wait_ge(s_trw, 136 * 16)
                    if tau == 2 and jc == 0 and k == 0:
                        t.wait_ge(s_trw, 200 * 16)
                    pj = prev_same_bank.get(j)
                    if pj is not None:
                        t.wait_ge(s_ev, evi[pj])
                    ps = (psA, psB)[j % 2]
                    for dc in range(8):
                        mi = t.matmul(ps[:, 0:tw], wT[:, buf, jc, dc * 128:(dc + 1) * 128],
                                      xn_fm[:, dc, t0:t0 + tw],
                                      start=(dc == 0), stop=(dc == 7))
                    mi.then_inc(s_mm, 1)
                elif it[0] == 'norm':
                    tau, k4 = it[1], it[2]
                    t0, tw = TT_TOK[k4]
                    t.wait_ge(s_ac, (6 if tau == 0 else 8))
                    if k4 == 0:
                        t.wait_ge(s_ac, 5)
                    pn = None
                    for jj in range(j - 1, -1, -1):
                        if sched[jj][0] == 'norm':
                            pn = jj
                            break
                    if pn is not None:
                        t.wait_ge(s_ev, evi[pn])
                    for fc in range(8):
                        mi = t.matmul(psN[:, 0:tw], ind16_sb[:, fc, :], raw[:, fc, t0:t0 + tw],
                                      start=(fc == 0), stop=(fc == 7))
                    mi.then_inc(s_mm, 1)
                else:
                    tau, slot, fc, k4 = it[1], it[2], it[3], it[4]
                    t0, tw = TT_TOK[k4]
                    need = {0: (4 if slot == 1 else 3), 1: 9, 2: 12}[tau]
                    t.wait_ge(s_vst, need)
                    pj = prev_same_bank.get(j)
                    if pj is not None:
                        t.wait_ge(s_ev, evi[pj])
                    ps = (psA, psB)[j % 2]
                    if tau == 2:
                        rhs = eta_sb[:, 128 + t0:128 + t0 + tw]
                    else:
                        rhs = scal_sb[:, slot, t0:t0 + tw]
                    t.matmul(ps[:, 0:tw], ind16T_sb[:, fc, :], rhs,
                             start=True, stop=True).then_inc(s_mm, 1)

        @block.vector
        def _(v):
            for i in range(NT):
                v.wait_ge(s_lna, 3 * i + 1)
                v.wait_ge((s_xld0, s_xld1)[i % 2], (i // 2 + 1) * 16)
                v.reduce_sum(stats[:, 0:1], x_db[:, i % 2, :], axis=AX.X)
                v.drain()
                v.tensor_scalar(stats[:, 2:3], stats[:, 0:1], 1.0 / DM, None, ALU.mult)
                v.drain()
                v.tensor_scalar(stats[:, 5:6], stats[:, 2:3], stats[:, 2:3], None, ALU.mult)
                v.tensor_scalar(stats[:, 6:7], stats[:, 1:2], 1.0 / DM, None, ALU.mult)
                v.drain()
                v.tensor_scalar(stats[:, 6:7], stats[:, 6:7], stats[:, 5:6], 1e-5,
                                ALU.subtract, ALU.add)
                v.drain()
                v.reciprocal(stats[:, 7:8], stats[:, 6:7])
                v.drain()
                v.engine_nop().then_inc(s_lnv, 1)
                v.wait_ge(s_lna, 3 * i + 2)
                v.tensor_scalar(stats[:, 4:5], stats[:, 2:3], stats[:, 3:4], -1.0,
                                ALU.mult, ALU.mult)
                v.drain()
                v.engine_nop().then_inc(s_lnv, 1)
            vst = 0
            for j, it in enumerate(sched):
                if it[0] == 'beta':
                    continue
                v.wait_ge(s_mm, j + 1)
                if it[0] == 'proj':
                    tau, jc, k = it[1], it[2], it[3]
                    t0, tw = TT_EXT[k]
                    ps = (psA, psB)[j % 2]
                    if jc == 0 and k == 0:
                        if tau == 1:
                            v.wait_ge(s_out, 32)
                        if tau == 2:
                            v.wait_ge(s_out, 48)
                    v.tensor_scalar(raw[:, jc, t0:t0 + tw], ps[:, 0:tw],
                                    biases[:, tau, jc:jc + 1], None, ALU.add
                                    ).then_inc(s_ev, 1)
                    if jc == 7 and k == len(TT_EXT) - 1:
                        v.drain()
                        v.tensor_scalar(raw[:, :, 125:128], raw[:, :, 125:128],
                                        hmask[:, 0:1], None, ALU.mult)
                        v.drain()
                        for fc in range(8):
                            v.tensor_scalar(f32scr[:], raw[:, fc, 125:125 + TOK],
                                            convw[:, tau, fc, 0:1], None, ALU.mult)
                            v.drain()
                            for jj in range(1, 4):
                                v.scalar_tensor_tensor(f32scr[:],
                                                       raw[:, fc, 125 + jj:125 + jj + TOK],
                                                       convw[:, tau, fc, jj:jj + 1],
                                                       f32scr[:],
                                                       ALU.mult, ALU.add)
                                v.drain()
                            v.tensor_copy(conv_o[:, fc, :], f32scr[:])
                            v.drain()
                        vst += 1
                        v.engine_nop().then_inc(s_vst, 1)
                        if tau == 2:
                            v.engine_nop().then_inc(s_vst, 1)
                elif it[0] == 'norm':
                    tau, k4 = it[1], it[2]
                    t0, tw = TT_TOK[k4]
                    v.tensor_copy(n2_sb[:, t0:t0 + tw], psN[:, 0:tw]).then_inc(s_ev, 1)
                    if k4 == len(TT_TOK) - 1:
                        v.drain()
                        v.engine_nop().then_inc(s_vst, 1)
                        v.wait_ge(s_ac, 7 if tau == 0 else 9)
                        v.tensor_scalar(f32scr[0:16, :], f32scr[0:16, :], 1e-6, None,
                                        ALU.add)
                        v.drain()
                        v.reciprocal(f32scr[0:16, :], f32scr[0:16, :])
                        v.drain()
                        v.tensor_copy(scal_sb[:, 0, :], f32scr[0:16, :])
                        v.drain()
                        if tau == 0:
                            v.scalar_tensor_tensor(scal_sb[:, 1, :], f32scr[0:16, :],
                                                   -1.0, eta_sb[:, 128:EXT],
                                                   ALU.mult, ALU.mult)
                            v.drain()
                        v.engine_nop().then_inc(s_vst, 1)
                        if tau == 0:
                            v.engine_nop().then_inc(s_vst, 1)
                else:
                    tau, slot, fc, k4 = it[1], it[2], it[3], it[4]
                    t0, tw = TT_TOK[k4]
                    ps = (psA, psB)[j % 2]
                    if tau == 2 and fc == 0 and k4 == 0:
                        v.wait_ge(s_ac, 10)
                    if tau == 0 and slot == 0:
                        dst = outb
                    elif tau == 0 and slot == 1:
                        dst = raw
                    elif tau == 1:
                        dst = outb
                    else:
                        dst = outb
                    if tau == 0 and slot == 1:
                        v.tensor_tensor(raw[:, fc, t0:t0 + tw], conv_o[:, fc, t0:t0 + tw],
                                        ps[:, 0:tw], ALU.mult).then_inc(s_ev, 1)
                    else:
                        v.tensor_tensor(dst[:, fc, t0:t0 + tw], conv_o[:, fc, t0:t0 + tw],
                                        ps[:, 0:tw], ALU.mult).then_inc(s_ev, 1)
                    if fc == 7 and k4 == len(TT_TOK) - 1:
                        v.drain()
                        if tau == 0 and slot == 0:
                            v.engine_nop().then_inc(s_vst, 1)
                        elif tau == 0 and slot == 1:
                            v.engine_nop().then_inc(s_vst, 1)
                        elif tau == 1:
                            v.engine_nop().then_inc(s_vst, 1)
                        else:
                            v.engine_nop().then_inc(s_vst, 1)
    es.close()
    return d, outs


def hostprep_B(inp, core, kh, bt, qh, vt, ncht=NCH_TOT):
    cc = chunk_consts()
    npairs = ncht // 2
    tmtT = np.zeros((128, npairs, CH), np.float32)
    for pair in range(npairs):
        for par in range(2):
            i = 2 * pair + par
            tmtT[64 * par:64 * par + 64, pair, :] = cc['Tmat'][i].T
    wcol = np.zeros((128, ncht), np.float32)
    acol = np.zeros((128, ncht), np.float32)
    ccol = np.zeros((128, ncht), np.float32)
    for i in range(ncht):
        for par in range(2):
            wcol[64 * par:64 * par + 64, i] = cc['w'][i]
            acol[64 * par:64 * par + 64, i] = cc['a'][i]
            ccol[64 * par:64 * par + 64, i] = cc['c'][i]
    striu = np.zeros((128, CH), np.float32)
    stril = np.zeros((128, CH), np.float32)
    identt = np.zeros((128, 128), np.float32)
    ident2 = np.zeros((128, CH), np.float32)
    for p in range(128):
        c1 = p % 64
        striu[p, c1 + 1:] = 1.0
        stril[p, :c1] = 1.0
        identt[p, c1] = 1.0
        identt[p, 64 + c1] = 1.0
        ident2[p, c1] = 1.0
    bf = ml_dtypes.bfloat16
    return {
        'kh_i': kh, 'bt_i': bt, 'qh_i': qh, 'vt_i': vt,
        'tmtT_p': tmtT.astype(bf), 'wcol': wcol, 'acol': acol, 'ccol': ccol,
        'striu': striu.astype(bf), 'stril': stril.astype(bf),
        'ident2': ident2.astype(bf), 'identt': identt.astype(bf),
        'rmsw_b': np.tile(np.asarray(inp['L0_attn_rms_w'])[None, :], (128, 1)).astype(np.float32),
    }


def build_phaseB(nc, ncht=NCH_TOT, win=12, stage=5, par_max=7, fx=None):
    import contextlib
    NP = ncht // 2
    T = ncht * CH
    TC = T // 128
    d = {}
    if fx is not None:
        d = fx['d']
        o_o = fx['o_o']
    else:
        for nm in ("kh_i", "bt_i", "qh_i", "vt_i"):
            d[nm] = nc.dram_tensor(nm, [128, T], BF16, kind="ExternalInput").ap()
        d['tmtT_p'] = nc.dram_tensor("tmtT_p", [128, NP, CH], BF16, kind="ExternalInput").ap()
        for nm in ("wcol", "acol", "ccol"):
            d[nm] = nc.dram_tensor(nm, [128, ncht], F32, kind="ExternalInput").ap()
        for nm in ("striu", "stril", "ident2"):
            d[nm] = nc.dram_tensor(nm, [128, CH], BF16, kind="ExternalInput").ap()
        d['identt'] = nc.dram_tensor("identt", [128, 128], BF16, kind="ExternalInput").ap()
        d['rmsw_b'] = nc.dram_tensor("rmsw_b", [128, CH], F32, kind="ExternalInput").ap()
        o_o = nc.dram_tensor("o_o", [128, TC, 128], BF16, kind="ExternalOutput").ap()

    cc = chunk_consts()

    es = contextlib.ExitStack()
    C = es.enter_context
    SB = _mk_sb(nc, C, fx)
    PS = _mk_ps(nc, C, fx)
    kh = SB([128, T], BF16)
    btl = SB([128, T], BF16)
    qh = SB([128, T], BF16)
    vt = SB([128, T], BF16)
    o_sb = SB([128, TC, 128], BF16)
    kt = SB([128, win, 2, CH], BF16)
    btt = SB([128, win, 2, CH], BF16)
    vtt = SB([128, win, 2, CH], BF16)
    tmtw = SB([128, win, CH], BF16)
    A_sb = SB([128, win, 128], BF16)
    At_sb = SB([128, win, 128], BF16)
    Xm_sb = SB([128, win, 128], BF16)
    JB = SB([128, win, 2, 192], BF16)
    Zn_sb = SB([128, win, 128], BF16)
    Y_sb = SB([128, win, 128], BF16)
    U_sb = SB([128, 2, 128], BF16)
    W_sb = SB([128, CH], F32)
    Wa_sb = SB([128, CH], F32)
    W_bf = SB([128, CH], BF16)
    Wa_bf = SB([128, CH], BF16)
    tc2w = SB([128, CH], F32)
    tmp1 = SB([128, 128], F32)
    sq_scr = SB([128, 128], F32)
    wcol_sb = SB([128, ncht], F32)
    acol_sb = SB([128, ncht], F32)
    ccol_sb = SB([128, ncht], F32)
    striu_sb = SB([128, CH], BF16)
    stril_sb = SB([128, CH], BF16)
    ident2_sb = SB([128, CH], BF16)
    identt_sb = SB([128, 128], BF16)
    rmsw_sb = SB([128, CH], F32)
    msc = SB([128, 2 * TC], F32)
    psG0 = PS(0, [128, 512])
    psG1 = PS(1, [128, 512])
    psJ = PS(2, [128, 512])
    psU0 = PS(3, [128, 512])
    psU1 = PS(4, [128, 512])
    psO0 = PS(5, [128, 512])
    psO1 = PS(6, [128, 512])
    psJB = PS(7, [128, 512])
    s_bld = C(nc.semaphore(f"B_s_bld"))
    s_trs = [C(nc.semaphore(f"B_s_tr2_{i}")) for i in range(win)]
    s_pmm = C(nc.semaphore(f"B_s_pmm"))
    s_pevG = C(nc.semaphore(f"B_s_pevG"))
    s_pevD = C(nc.semaphore(f"B_s_pevD"))
    s_pevZ = C(nc.semaphore(f"B_s_pevZ"))
    s_smm = C(nc.semaphore(f"B_s_smm"))
    s_sev = C(nc.semaphore(f"B_s_sev"))
    s_sac = C(nc.semaphore(f"B_s_sac"))
    s_srl = C(nc.semaphore(f"B_s_srl"))
    s_wav = C(nc.semaphore(f"B_s_wav"))
    s_oev = C(nc.semaphore(f"B_s_oev"))
    s_rms = C(nc.semaphore(f"B_s_rms"))
    s_out = C(nc.semaphore(f"B_s_out"))
    block = fx['block'] if fx is not None else C(nc.Block())
    NBLD = 12 * 16
    if True:
        GA = 2
        LAG = 4
        act_idx = {}
        pmm_idx = {}
        _ka = 0
        _kp = 0
        for _w in range(NP + LAG):
            if _w < NP:
                _kp += 1
                pmm_idx[(_w, 0)] = _kp
            _q = _w - GA
            if 0 <= _q < NP:
                for _s in range(1, 8):
                    _kp += 1
                    pmm_idx[(_q, _s)] = _kp
                for _s in range(1, 7):
                    _ka += 1
                    act_idx[(_q, _s)] = _ka
        psG = (psG0, psG1)
        psJs = (psJ, psJB)
        psU = (psU0, psU1)
        psO = (psO0, psO1)

        @block.gpsimd
        def _(g):
            g.dma_start(out=wcol_sb[:], in_=d['wcol']).then_inc(s_bld, 16)
            g.dma_start(out=acol_sb[:], in_=d['acol']).then_inc(s_bld, 16)
            g.dma_start(out=ccol_sb[:], in_=d['ccol']).then_inc(s_bld, 16)
            g.dma_start(out=striu_sb[:], in_=d['striu']).then_inc(s_bld, 16)
            g.dma_start(out=stril_sb[:], in_=d['stril']).then_inc(s_bld, 16)
            g.dma_start(out=ident2_sb[:], in_=d['ident2']).then_inc(s_bld, 16)
            g.dma_start(out=identt_sb[:], in_=d['identt']).then_inc(s_bld, 16)
            g.dma_start(out=rmsw_sb[:], in_=d['rmsw_b']).then_inc(s_bld, 16)
            if fx is not None:
                fx['b_loads'](g, dict(kh=kh, btl=btl, qh=qh, vt=vt), s_bld)
            else:
                g.dma_start(out=kh[:], in_=d['kh_i']).then_inc(s_bld, 16)
                g.dma_start(out=btl[:], in_=d['bt_i']).then_inc(s_bld, 16)
                g.dma_start(out=qh[:], in_=d['qh_i']).then_inc(s_bld, 16)
                g.dma_start(out=vt[:], in_=d['vt_i']).then_inc(s_bld, 16)
            if stage >= 5:
                g.wait_ge(s_rms, 4)
            elif stage == 4:
                g.wait_ge(s_oev, ncht)
            elif stage == 3:
                g.wait_ge(s_srl, ncht)
            elif stage == 2:
                g.wait_ge(s_sev, ncht)
            elif stage == 1:
                g.wait_ge(s_pevZ, NP)
            g.dma_start(out=o_o, in_=o_sb[:]).then_inc(s_out, 16)
            g.wait_ge(s_out, 16)

        @block.sync
        def _(s):
            s.wait_ge(s_bld, NBLD)
            for p in (range(NP) if stage >= 1 else []):
                w = p % win
                if p >= win:
                    s.wait_ge(s_smm, 8 * (p - win) + 8)
                for h in range(2):
                    H = 64 * h
                    s.dma_start(out=kt[:, w, h, :], in_=kh[H:H + 64, 128 * p:128 * (p + 1)],
                                transpose=True).then_inc(s_trs[w], 16)
                    s.dma_start(out=btt[:, w, h, :], in_=btl[H:H + 64, 128 * p:128 * (p + 1)],
                                transpose=True).then_inc(s_trs[w], 16)
                    s.dma_start(out=vtt[:, w, h, :], in_=vt[H:H + 64, 128 * p:128 * (p + 1)],
                                transpose=True).then_inc(s_trs[w], 16)
                s.dma_start(out=tmtw[:, w, :], in_=d['tmtT_p'][:, p, :]).then_inc(s_trs[w], 16)

        @block.tensor
        def _(t):
            t.wait_ge(s_bld, NBLD)

            def par_stage(p, stg):
                if stage < 1 or stg > par_max:
                    return
                w = p % win
                if stg == 0:
                    t.wait_ge(s_trs[w], (p // win + 1) * 7 * 16)
                    if p > 0:
                        t.wait_ge(s_pevG, p)
                    if p >= win:
                        t.wait_ge(s_smm, 8 * (p - win) + 8)
                    for par in range(2):
                        P = 64 * par
                        cs = slice(64 * (2 * p + par), 64 * (2 * p + par + 1))
                        for h in range(2):
                            H = 64 * h
                            ps = psG[h]
                            t.matmul(ps[P:P + 64, 0:64], kh[H:H + 64, cs], btl[H:H + 64, cs],
                                     start=True, stop=True, tile_position=(H, P))
                            t.matmul(ps[P:P + 64, 64:128], btl[H:H + 64, cs], kh[H:H + 64, cs],
                                     start=True, stop=True, tile_position=(H, P))
                            mi = t.matmul(ps[P:P + 64, 128:192], kh[H:H + 64, cs], qh[H:H + 64, cs],
                                          start=True, stop=True, tile_position=(H, P))
                    mi.then_inc(s_pmm, 1)
                elif stg == 1:
                    psJw = psJs[p % 2]
                    t.wait_ge(s_pevG, p + 1)
                    if p >= 2:
                        t.wait_ge(s_pevZ, p - 1)
                    if 2 * p - 4 > 0:
                        t.wait_ge(s_oev, 2 * p - 4)
                    for par in range(2):
                        P = 64 * par
                        for h in range(2):
                            hc = slice(64 * h, 64 * h + 64)
                            base = 192 * h
                            t.matmul(psJw[P:P + 64, base + 64:base + 128],
                                     At_sb[P:P + 64, w, hc], A_sb[P:P + 64, w, hc],
                                     start=True, stop=True, tile_position=(P, P))
                            mi = t.matmul(psJw[P:P + 64, base + 128:base + 192],
                                          A_sb[P:P + 64, w, hc], At_sb[P:P + 64, w, hc],
                                          start=True, stop=True, tile_position=(P, P))
                    mi.then_inc(s_pmm, 1)
                elif stg in (2, 3, 4, 5):
                    psJw = psJs[p % 2]
                    t.wait_ge(s_pevD, act_idx[(p, stg - 1)])
                    for par in range(2):
                        P = 64 * par
                        for h in range(2):
                            base = 192 * h
                            t.matmul(psJw[P:P + 64, base:base + 128],
                                     JB[P:P + 64, w, h, 128:192], JB[P:P + 64, w, h, 0:128],
                                     start=True, stop=False, tile_position=(P, P))
                            t.matmul(psJw[P:P + 64, base:base + 64],
                                     ident2_sb[P:P + 64, :], JB[P:P + 64, w, h, 0:64],
                                     start=False, stop=True, tile_position=(P, P))
                            mi = t.matmul(psJw[P:P + 64, base + 128:base + 192],
                                          JB[P:P + 64, w, h, 64:128], JB[P:P + 64, w, h, 128:192],
                                          start=True, stop=True, tile_position=(P, P))
                    mi.then_inc(s_pmm, 1)
                elif stg == 6:
                    psJw = psJs[p % 2]
                    t.wait_ge(s_pevD, act_idx[(p, 5)])
                    for par in range(2):
                        P = 64 * par
                        for h in range(2):
                            base = 192 * h
                            t.matmul(psJw[P:P + 64, base:base + 64],
                                     JB[P:P + 64, w, h, 128:192], JB[P:P + 64, w, h, 0:64],
                                     start=True, stop=False, tile_position=(P, P))
                            mi = t.matmul(psJw[P:P + 64, base:base + 64],
                                          ident2_sb[P:P + 64, :], JB[P:P + 64, w, h, 0:64],
                                          start=False, stop=True, tile_position=(P, P))
                    mi.then_inc(s_pmm, 1)
                else:
                    psJw = psJs[p % 2]
                    t.wait_ge(s_pevD, act_idx[(p, 6)])
                    for par in range(2):
                        P = 64 * par
                        for h in range(2):
                            H = 64 * h
                            t.matmul(psJw[H:H + 64, 64 * par:64 * par + 64],
                                     btt[P:P + 64, w, h, :], JB[P:P + 64, w, h, 0:64],
                                     start=True, stop=True, tile_position=(P, H))
                            mi = t.matmul(psJw[P:P + 64, 384 + 64 * h:384 + 64 * h + 64],
                                          JB[P:P + 64, w, h, 0:64], vtt[P:P + 64, w, h, :],
                                          start=True, stop=True, tile_position=(P, P))
                        if par == 0:
                            t.drain()
                    mi.then_inc(s_pmm, 1)

            def ser_group(i, grp):
                p = i // 2
                par = i % 2
                P = 64 * par
                w = p % win
                cs = slice(64 * i, 64 * (i + 1))
                if grp == 0:
                    t.wait_ge(s_pevZ, p + 1)
                    if i > 0:
                        t.wait_ge(s_srl, i)
                    for h in range(2):
                        H = 64 * h
                        mi = t.matmul(psU[h][P:P + 64, 0:64],
                                      Zn_sb[H:H + 64, w, 64 * par:64 * par + 64],
                                      W_bf[H:H + 64, :],
                                      start=True, stop=True, tile_position=(H, P))
                    mi.then_inc(s_smm, 1)
                elif grp == 1:
                    t.wait_ge(s_wav, i)
                    t.wait_ge(s_oev, i)
                    for h in range(2):
                        H = 64 * h
                        t.matmul(psO[h][P:P + 64, 0:64],
                                 qh[H:H + 64, cs], W_bf[H:H + 64, :],
                                 start=True, stop=True, tile_position=(H, P))
                        mi = t.matmul(psO[h][P:P + 64, 64:128],
                                      qh[H:H + 64, cs], Wa_bf[H:H + 64, :],
                                      start=True, stop=True, tile_position=(H, P))
                    mi.then_inc(s_smm, 1)
                elif grp == 2:
                    psSer = psJs[(p + 1) % 2]
                    t.wait_ge(s_sev, 2 * i + 2)
                    if p + 2 <= NP:
                        t.wait_ge(s_pevZ, min(p + 2, NP))
                    for h in range(2):
                        H = 64 * h
                        mi = t.matmul(psSer[H:H + 64, 0:64],
                                      kt[P:P + 64, w, h, :],
                                      U_sb[P:P + 64, 0, 64 * h:64 * h + 64],
                                      start=True, stop=True, tile_position=(P, H))
                    mi.then_inc(s_smm, 1)
                else:
                    psSer = psJs[(p + 1) % 2]
                    t.wait_ge(s_sac, 3 * i + 1)
                    t.wait_ge(s_srl, i + 1)
                    for h in range(2):
                        H = 64 * h
                        t.matmul(psSer[H:H + 64, 64:128],
                                 kt[P:P + 64, w, h, :], U_sb[P:P + 64, 1, 64 * h:64 * h + 64],
                                 start=True, stop=True, tile_position=(P, H))
                        mi = t.matmul(psSer[P:P + 64, 128 + 64 * h:128 + 64 * h + 64],
                                      Xm_sb[P:P + 64, w, 64 * h:64 * h + 64],
                                      U_sb[P:P + 64, 0, 64 * h:64 * h + 64],
                                      start=True, stop=True, tile_position=(P, P))
                    mi.then_inc(s_smm, 1)

            for w in range(NP + LAG):
                if w < NP:
                    par_stage(w, 0)
                pD = w - GA
                pS = w - LAG
                ser = []
                if 0 <= pS < NP:
                    ser = [(2 * pS, g) for g in range(4)] + \
                          [(2 * pS + 1, g) for g in range(4)]
                if 0 <= pD < NP:
                    k = 0
                    for s in range(1, 8):
                        par_stage(pD, s)
                        for _ in range(2 if s == 7 else 1):
                            if k < len(ser):
                                ser_group(*ser[k])
                                k += 1
                else:
                    for it in ser:
                        ser_group(*it)

        @block.vector
        def _(v):
            v.wait_ge(s_bld, NBLD)
            v.memset(W_sb[:], 0.0)
            v.memset(Wa_sb[:], 0.0)
            v.memset(W_bf[:], 0.0)
            v.memset(Wa_bf[:], 0.0)
            v.drain()

            def par_evac_g(p):
                if stage < 1:
                    return
                w = p % win
                v.wait_ge(s_pmm, pmm_idx[(p, 0)])
                for h in range(2):
                    hc = slice(64 * h, 64 * h + 64)
                    v.tensor_tensor(A_sb[:, w, hc], psG[h][:, 0:64], striu_sb[:], ALU.mult)
                    v.tensor_tensor(At_sb[:, w, hc], psG[h][:, 64:128], stril_sb[:], ALU.mult)
                    v.tensor_tensor(Xm_sb[:, w, hc], psG[h][:, 128:192], tmtw[:, w, :], ALU.mult)
                v.drain()
                v.tensor_tensor(JB[:, w, :, 0:64],
                                A_sb[:, w, :].rearrange("p (h c) -> p h c", h=2),
                                identt_sb[:].rearrange("p (h c) -> p h c", h=2),
                                ALU.add).then_inc(s_pevG, 1)
            def ser_evac(i):
                p = i // 2
                par = i % 2
                P = 64 * par
                if stage < 2:
                    return
                v.wait_ge(s_smm, 4 * i + 1)
                for h in range(2):
                    v.tensor_tensor(U_sb[P:P + 64, 0, 64 * h:64 * h + 64],
                                    Y_sb[P:P + 64, p % win, 64 * h:64 * h + 64],
                                    psU[h][P:P + 64, 0:64], ALU.add).then_inc(s_sev, 1)
                if stage == 2:
                    v.engine_nop().then_inc(s_srl, 1)
                    return
                c2 = float(cc['coef2'][i])
                c1 = float(cc['coef1'][i])
                psSer = psJs[(p + 1) % 2]
                v.tensor_scalar(tc2w[:], W_sb[:], c2, None, ALU.mult)
                v.wait_ge(s_smm, 4 * i + 3)
                v.tensor_tensor(W_sb[:], W_sb[:], psSer[:, 0:64], ALU.add)
                v.drain()
                v.tensor_copy(W_bf[:], W_sb[:]).then_inc(s_srl, 1)
                if stage == 3:
                    return
                v.wait_ge(s_smm, 4 * i + 4)
                v.scalar_tensor_tensor(Wa_sb[:], Wa_sb[:], c1, tc2w[:], ALU.mult, ALU.add)
                v.drain()
                v.tensor_tensor(Wa_sb[:], Wa_sb[:], psSer[:, 64:128], ALU.add)
                v.drain()
                v.tensor_copy(Wa_bf[:], Wa_sb[:]).then_inc(s_wav, 1)
                v.wait_ge(s_sac, 3 * i + 3)
                tcc = i // 2
                for h in range(2):
                    v.scalar_tensor_tensor(tmp1[P:P + 64, 64 * h:64 * h + 64],
                                           psO[h][P:P + 64, 64:128],
                                           acol_sb[P:P + 64, i:i + 1],
                                           tmp1[P:P + 64, 64 * h:64 * h + 64],
                                           ALU.mult, ALU.add)
                v.drain()
                v.tensor_tensor(o_sb[P:P + 64, tcc, 0:64],
                                tmp1[P:P + 64, 0:64],
                                psSer[P:P + 64, 128:192], ALU.add)
                v.tensor_tensor(o_sb[P:P + 64, tcc, 64:128],
                                tmp1[P:P + 64, 64:128],
                                psSer[P:P + 64, 192:256], ALU.add).then_inc(s_oev, 1)

            for w in range(NP + LAG):
                if w < NP:
                    par_evac_g(w)
                pS = w - LAG
                if 0 <= pS < NP:
                    ser_evac(2 * pS)
                    ser_evac(2 * pS + 1)
            if stage < 5:
                return
            v.wait_ge(s_rms, 1)
            v.tensor_scalar(msc[:, 0:2 * TC], msc[:, 0:2 * TC], 1.0 / CH, 1e-6,
                            ALU.mult, ALU.add).then_inc(s_rms, 1)
            v.wait_ge(s_rms, 3)
            v.reciprocal(msc[:, 0:2 * TC], msc[:, 0:2 * TC])
            v.drain()
            mi = None
            for tcc in range(TC):
                for h in range(2):
                    mi = v.scalar_tensor_tensor(o_sb[:, tcc, 64 * h:64 * h + 64],
                                                o_sb[:, tcc, 64 * h:64 * h + 64],
                                                msc[:, 2 * tcc + h:2 * tcc + h + 1],
                                                rmsw_sb[:], ALU.mult, ALU.mult)
            mi.then_inc(s_rms, 1)

        @block.scalar
        def _(a):
            a.wait_ge(s_bld, NBLD)

            def act_par_stage(p, s):
                if stage < 1 or par_max < s:
                    return
                w = p % win
                psJw = psJs[p % 2]
                if s == 1:
                    a.wait_ge(s_pmm, pmm_idx[(p, 1)])
                    a.activation(out=JB[:, w, :, 64:192],
                                 in_=psJw[:, 0:384].rearrange("p (h c) -> p h c", h=2)[:, :, 64:192],
                                 func=ACT.Copy).then_inc(s_pevD, 1)
                elif s in (2, 3, 4, 5):
                    a.wait_ge(s_pmm, pmm_idx[(p, s)])
                    a.activation(out=JB[:, w, :, 0:192],
                                 in_=psJw[:, 0:384].rearrange("p (h c) -> p h c", h=2),
                                 func=ACT.Copy).then_inc(s_pevD, 1)
                else:
                    a.wait_ge(s_pmm, pmm_idx[(p, 6)])
                    a.activation(out=JB[:, w, :, 0:64],
                                 in_=psJw[:, 0:384].rearrange("p (h c) -> p h c", h=2)[:, :, 0:64],
                                 func=ACT.Copy).then_inc(s_pevD, 1)

            def act_ser(i):
                if stage < 4:
                    return
                par = i % 2
                P = 64 * par
                a.wait_ge(s_sev, 2 * i + 2)
                a.activation(out=U_sb[P:P + 64, 1, :], in_=U_sb[P:P + 64, 0, :],
                             func=ACT.Copy, scale=wcol_sb[P:P + 64, i:i + 1]
                             ).then_inc(s_sac, 1)
                a.wait_ge(s_smm, 4 * i + 2)
                for h in range(2):
                    a.activation(out=tmp1[P:P + 64, 64 * h:64 * h + 64],
                                 in_=psO[h][P:P + 64, 0:64],
                                 func=ACT.Copy, scale=ccol_sb[P:P + 64, i:i + 1]
                                 ).then_inc(s_sac, 1)
                    a.drain()

            def act_evac_zy(p):
                if stage < 1 or par_max < 7:
                    return
                w2 = p % win
                psJw = psJs[p % 2]
                a.wait_ge(s_pmm, pmm_idx[(p, 7)])
                a.activation(out=Zn_sb[:, w2, :], in_=psJw[:, 0:128], func=ACT.Copy)
                a.activation(out=Y_sb[:, w2, :], in_=psJw[:, 384:512],
                             func=ACT.Copy).then_inc(s_pevZ, 1)

            for w in range(NP + LAG):
                pD = w - GA
                pS = w - LAG
                if 0 <= pD < NP:
                    for s in (1, 2, 3):
                        act_par_stage(pD, s)
                if 0 <= pS < NP:
                    act_ser(2 * pS)
                if 0 <= pD < NP:
                    for s in (4, 5, 6):
                        act_par_stage(pD, s)
                if 0 <= pS < NP:
                    act_ser(2 * pS + 1)
                if 0 <= pD < NP:
                    act_evac_zy(pD)
                q = pS - 2
                if 0 <= q < NP and stage >= 5:
                    a.wait_ge(s_oev, 2 * q + 2)
                    for h in range(2):
                        a.activation(out=sq_scr[:, 64 * h:64 * h + 64],
                                     in_=o_sb[:, q, 64 * h:64 * h + 64],
                                     func=ACT.Square,
                                     accum_out=msc[:, 2 * q + h:2 * q + h + 1])
                    a.drain()
            if stage < 5:
                return
            for q in range(NP - 2, NP):
                a.wait_ge(s_oev, 2 * q + 2)
                for h in range(2):
                    a.activation(out=sq_scr[:, 64 * h:64 * h + 64],
                                 in_=o_sb[:, q, 64 * h:64 * h + 64],
                                 func=ACT.Square,
                                 accum_out=msc[:, 2 * q + h:2 * q + h + 1])
                a.drain()
            a.nop().then_inc(s_rms, 1)
            a.wait_ge(s_rms, 2)
            a.drain()
            a.activation(out=msc[:, 0:2 * TC], in_=msc[:, 0:2 * TC],
                         func=ACT.Sqrt).then_inc(s_rms, 1)
    es.close()
    return d, o_o


def hostprep_C(inp, core, xn_i, o_i):
    x = np.asarray(inp['L0_attn_x'])[0]
    return {
        'xn_i': xn_i,
        'o_i': o_i,
        'x_i': np.ascontiguousarray(x[core * TOK:(core + 1) * TOK]),
        'router_w': np.asarray(inp['L0_ffn_router_w']),
        'w1': np.asarray(inp['L0_ffn_w1']),
        'w2': np.asarray(inp['L0_ffn_w2']),
        'Wout': np.asarray(inp['L0_attn_Wout']),
        'bout_bc': np.tile(np.asarray(inp['L0_attn_bout'])[None, :], (128, 1)).astype(np.float32),
        'iota8': np.tile(np.arange(8, dtype=np.float32), (128, 1)),
    }


def build_phaseC(nc, fx=None):
    import contextlib
    d = {}
    if fx is not None:
        d = fx['d']
        out_o = fx['out_o']
        wobf = fx['wobf']
        yhbm = fx['yhbm']
    else:
        d['xn_i'] = nc.dram_tensor("xn_i", [128, 8, TOK], BF16, kind="ExternalInput").ap()
        d['o_i'] = nc.dram_tensor("o_i", [128, 16, DM], BF16, kind="ExternalInput").ap()
        d['x_i'] = nc.dram_tensor("x_i", [TOK, DM], F32, kind="ExternalInput").ap()
        d['router_w'] = nc.dram_tensor("router_w", [DM, E], F32, kind="ExternalInput").ap()
        d['w1'] = nc.dram_tensor("w1", [E, DM, FF], F32, kind="ExternalInput").ap()
        d['w2'] = nc.dram_tensor("w2", [E, FF, DM], F32, kind="ExternalInput").ap()
        d['Wout'] = nc.dram_tensor("Wout", [DM, DM], F32, kind="ExternalInput").ap()
        d['bout_bc'] = nc.dram_tensor("bout_bc", [128, DM], F32, kind="ExternalInput").ap()
        d['iota8'] = nc.dram_tensor("iota8", [128, E], F32, kind="ExternalInput").ap()
        out_o = nc.dram_tensor("out_o", [TOK, DM], F32, kind="ExternalOutput").ap()
        wobf = nc.dram_tensor("wobf", [DM, DM], BF16, kind="Internal").ap()
        yhbm = nc.dram_tensor("yhbm", [2, 128, 8, DM], F32, kind="Internal").ap()

    es = contextlib.ExitStack()
    C = es.enter_context
    SB = _mk_sb(nc, C, fx)
    PS = _mk_ps(nc, C, fx)
    big = SB([128, 8 * TOK], BF16)
    xn_fm = big[:].rearrange("p (a b) -> p a b", a=8)
    o_fm = xn_fm
    y_sb = SB([128, 8, DM], F32)
    h_sb = SB([128, 16, 512], BF16)
    w1_sb = SB([128, 8, FF], BF16)
    w2_sb = SB([128, 16, DM], BF16)
    woT = SB([128, 8, DM], BF16)
    rw_sb = SB([128, 8, E], BF16)
    cw_sb = SB([128, 16, E], F32)
    iota_sb = SB([128, E], F32)
    bout_sb = SB([128, DM], F32)
    lg = SB([128, E], F32)
    m8 = SB([128, 8], F32)
    idxu = SB([128, 8], U32)
    idxf = SB([128, 8], F32)
    oh1 = SB([128, E], F32)
    oh2 = SB([128, E], F32)
    rst = SB([128, 4], F32)
    x_ep = SB([128, 2, DM], F32)
    y_ep = SB([128, 2, DM], F32)
    g_ep = SB([128, 2, DM], F32)
    s_tmp = SB([128, DM], F32)
    out_sb = SB([128, 2, DM], F32)
    psH0 = PS(0, [128, 512])
    psH1 = PS(1, [128, 512])
    psY0 = PS(2, [128, 512])
    psY1 = PS(3, [128, 512])
    psL = PS(4, [128, 512])
    psH2 = PS(5, [128, 512])
    psH3 = PS(6, [128, 512])
    psY2 = PS(7, [128, 512])
    psHs = (psH0, psH1, psH2, psH3)
    psYs = (psY0, psY1, psY2)
    s_cld = C(nc.semaphore(f"C_s_cld"))
    s_w1 = C(nc.semaphore(f"C_s_w1"))
    s_w2 = C(nc.semaphore(f"C_s_w2"))
    s_lmm = C(nc.semaphore(f"C_s_lmm"))
    s_rt0 = C(nc.semaphore(f"C_s_rt0"))
    s_rt1 = C(nc.semaphore(f"C_s_rt1"))
    s_rt2 = C(nc.semaphore(f"C_s_rt2"))
    s_cw = C(nc.semaphore(f"C_s_cw"))
    s_hmm = C(nc.semaphore(f"C_s_hmm"))
    s_hev = C(nc.semaphore(f"C_s_hev"))
    s_ymm = C(nc.semaphore(f"C_s_ymm"))
    s_yev = C(nc.semaphore(f"C_s_yev"))
    s_ysp = C(nc.semaphore(f"C_s_ysp"))
    s_otr = C(nc.semaphore(f"C_s_otr"))
    s_wot = C(nc.semaphore(f"C_s_wot"))
    s_smm3 = C(nc.semaphore(f"C_s_smm3"))
    s_sev3 = C(nc.semaphore(f"C_s_sev3"))
    s_yld = [C(nc.semaphore(f"C_s_yld{i}")) for i in range(2)]
    s_xld2 = [C(nc.semaphore(f"C_s_xld2{i}")) for i in range(2)]
    s_gs = C(nc.semaphore(f"C_s_gs"))
    s_ep = C(nc.semaphore(f"C_s_ep"))
    s_outp = [C(nc.semaphore(f"C_s_outp{i}")) for i in range(2)]
    block = fx['block'] if fx is not None else C(nc.Block())
    if True:
        NE = 16

        @block.gpsimd
        def _(g):
            g.dma_start(out=xn_fm[:, :, :],
                        in_=(fx['xn_src'] if fx is not None else d['xn_i'])
                        ).then_inc(s_cld, 16)
            g.dma_start(out=rw_sb[:],
                        in_=d['router_w'].rearrange("(dc dp) e -> dp dc e", dp=128)
                        ).then_inc(s_cld, 16)
            g.dma_start(out=iota_sb[:], in_=d['iota8']).then_inc(s_cld, 16)
            g.dma_start(out=bout_sb[:], in_=d['bout_bc']).then_inc(s_cld, 16)
            g.dma_start(out=wobf[:], in_=d['Wout']).then_inc(s_cld, 16)
            for ei in range(NE):
                e = ei % 8
                if ei > 0:
                    g.wait_ge(s_hmm, 32 * ei)
                g.dma_start(out=w1_sb[:],
                            in_=d['w1'][e].rearrange("(dc dp) f -> dp dc f", dp=128)
                            ).then_inc(s_w1, 16)
                if ei > 0:
                    g.wait_ge(s_ymm, 16 * ei)
                g.dma_start(out=w2_sb[:],
                            in_=d['w2'][e].rearrange("(fc fp) dm -> fp fc dm", fp=128)
                            ).then_inc(s_w2, 16)
                if ei == 4 and fx is not None and 'o_gather' in fx:
                    fx['o_gather'](g)
            g.wait_ge(s_outp[0], 8 * 16)
            g.wait_ge(s_outp[1], 8 * 16)

        @block.sync
        def _(s):
            s.wait_ge(s_cld, 5 * 16)
            for fc in range(8):
                for db in range(8):
                    s.dma_start(out=woT[:, fc, db * 128:(db + 1) * 128],
                                in_=wobf[db * 128:(db + 1) * 128, fc * 128:(fc + 1) * 128],
                                transpose=True).then_inc(s_wot, 16)
            for half in range(2):
                s.wait_ge(s_yev, 128 * (half + 1))
                s.dma_start(out=yhbm[half], in_=y_sb[:]).then_inc(s_ysp, 16)
            for tt in range(16):
                par = tt % 2
                if tt >= 2:
                    s.wait_ge(s_ep, tt - 2 + 1)
                s.wait_ge(s_ysp, 16 * (tt // 8 + 1))
                s.dma_start(out=y_ep[:, par, :],
                            in_=yhbm[tt // 8, :, (tt % 8), :]).then_inc(s_yld[par], 16)
                s.dma_start(out=x_ep[:, par, :],
                            in_=d['x_i'][128 * tt:128 * (tt + 1), :]).then_inc(s_xld2[par], 16)

        @block.scalar
        def _(a):
            for tt in range(16):
                a.wait_ge(s_rt1, tt + 1)
                a.drain()
                a.activation(out=lg[:], in_=lg[:], func=ACT.Exp).then_inc(s_rt2, 1)

            def o_transposes(tcs):
                for tc in tcs:
                    for fc in range(8):
                        a.dma_start(out=o_fm[:, fc, tc * 128:(tc + 1) * 128],
                                    in_=(fx['o_src'](tc, fc) if fx is not None
                                         else d['o_i'][:, tc, fc * 128:(fc + 1) * 128]),
                                    transpose=True).then_inc(s_otr, 16)

            for ei in range(NE):
                for t2 in range(2):
                    base = ei * 32 + t2 * 16
                    fc = 0
                    for bs in (4, 4, 4, 2, 2):
                        a.wait_ge(s_hmm, base + fc + bs)
                        for j in range(bs):
                            gidx = base + fc
                            a.activation(out=h_sb[:, fc, :],
                                         in_=psHs[gidx % 4][:],
                                         func=ACT.Silu).then_inc(s_hev, 1)
                            fc += 1
                if ei == 7:
                    a.wait_ge(s_hmm, 8 * 32)
                    if fx is not None and 's_cc2' in fx:
                        a.wait_ge(fx['s_cc2'], 17)
                    o_transposes(range(8))
            a.wait_ge(s_hmm, NE * 32)
            if fx is not None and 's_cc2' in fx:
                a.wait_ge(fx['s_cc2'], 17)
            o_transposes(range(8, 16))
            for tt in range(16):
                par = tt % 2
                a.wait_ge(s_yld[par], (tt // 2 + 1) * 16)
                a.drain()
                a.activation(out=g_ep[:, par, :], in_=y_ep[:, par, :],
                             func=ACT.Silu).then_inc(s_gs, 1)
                if tt >= 2:
                    a.wait_ge(s_ep, tt - 2 + 1)
                    a.dma_start(out=out_o[128 * (tt - 2):128 * (tt - 1), :],
                                in_=out_sb[:, tt % 2, :]).then_inc(s_outp[tt % 2], 16)
            for tt in (14, 15):
                a.wait_ge(s_ep, tt + 1)
                a.dma_start(out=out_o[128 * tt:128 * (tt + 1), :],
                            in_=out_sb[:, tt % 2, :]).then_inc(s_outp[tt % 2], 16)

        @block.tensor
        def _(t):
            t.wait_ge(s_cld, 5 * 16)
            for tt in range(16):
                if tt > 0:
                    t.wait_ge(s_rt0, tt)
                for dc in range(8):
                    mi = t.matmul(psL[:, 0:E], xn_fm[:, dc, tt * 128:(tt + 1) * 128],
                                  rw_sb[:, dc, :], start=(dc == 0), stop=(dc == 7))
                mi.then_inc(s_lmm, 1)
            hg = 0
            yg = 0
            for ei in range(NE):
                half = ei // 8
                t.wait_ge(s_w1, (ei + 1) * 16)
                for t2 in range(2):
                    tbase = half * 1024 + t2 * 512
                    for fc in range(16):
                        if hg >= 4:
                            t.wait_ge(s_hev, hg - 3)
                        ps = psHs[hg % 4]
                        for dc in range(8):
                            mi = t.matmul(ps[:],
                                          w1_sb[:, dc, fc * 128:(fc + 1) * 128],
                                          xn_fm[:, dc, tbase:tbase + 512],
                                          start=(dc == 0), stop=(dc == 7))
                        mi.then_inc(s_hmm, 1)
                        hg += 1
                    t.wait_ge(s_w2, (ei + 1) * 16)
                    t.wait_ge(s_hev, ei * 32 + t2 * 16 + 16)
                    for tt4 in range(4):
                        for dh in range(2):
                            if yg >= 3:
                                t.wait_ge(s_yev, yg - 2)
                            ps = psYs[yg % 3]
                            for fc in range(16):
                                mi = t.matmul(ps[:],
                                              h_sb[:, fc, tt4 * 128:(tt4 + 1) * 128],
                                              w2_sb[:, fc, dh * 512:(dh + 1) * 512],
                                              start=(fc == 0), stop=(fc == 15))
                            mi.then_inc(s_ymm, 1)
                            yg += 1
            t.wait_ge(s_wot, 64 * 16)
            t.wait_ge(s_yev, NE * 16)
            sg = 0
            for tt in range(16):
                t.wait_ge(s_otr, (tt + 1) * 8 * 16)
                for dh in range(2):
                    if sg >= 3:
                        t.wait_ge(s_sev3, sg - 2)
                    ps = psYs[sg % 3]
                    for fc in range(8):
                        mi = t.matmul(ps[:],
                                      o_fm[:, fc, tt * 128:(tt + 1) * 128],
                                      woT[:, fc, dh * 512:(dh + 1) * 512],
                                      start=(fc == 0), stop=(fc == 7))
                    mi.then_inc(s_smm3, 1)
                    sg += 1

        @block.vector
        def _(v):
            for tt in range(16):
                v.wait_ge(s_lmm, tt + 1)
                v.tensor_copy(lg[:], psL[:, 0:E]).then_inc(s_rt0, 1)
                v.drain()
                v.reduce_max(rst[:, 0:1], lg[:], axis=AX.X)
                v.drain()
                v.tensor_scalar(lg[:], lg[:], rst[:, 0:1], None, ALU.subtract
                                ).then_inc(s_rt1, 1)
                v.wait_ge(s_rt2, tt + 1)
                v.drain()
                v.reduce_sum(rst[:, 1:2], lg[:], axis=AX.X)
                v.drain()
                v.reciprocal(rst[:, 2:3], rst[:, 1:2])
                v.drain()
                v.tensor_scalar(lg[:], lg[:], rst[:, 2:3], None, ALU.mult)
                v.drain()
                v.max(m8[:], lg[:])
                v.drain()
                v.max_index(idxu[:], m8[:], lg[:])
                v.drain()
                v.tensor_copy(idxf[:, 0:2], idxu[:, 0:2])
                v.drain()
                v.tensor_scalar(oh1[:], iota_sb[:], idxf[:, 0:1], None, ALU.is_equal)
                v.tensor_scalar(oh2[:], iota_sb[:], idxf[:, 1:2], None, ALU.is_equal)
                v.tensor_scalar(rst[:, 3:4], m8[:, 0:1], m8[:, 1:2], 1e-9,
                                ALU.add, ALU.max)
                v.drain()
                v.reciprocal(rst[:, 3:4], rst[:, 3:4])
                v.drain()
                v.tensor_scalar(oh1[:], oh1[:], m8[:, 0:1], None, ALU.mult)
                v.drain()
                v.scalar_tensor_tensor(oh2[:], oh2[:], m8[:, 1:2], oh1[:],
                                       ALU.mult, ALU.add)
                v.drain()
                v.tensor_scalar(cw_sb[:, tt, :], oh2[:], rst[:, 3:4], None, ALU.mult)
                v.drain()
                v.engine_nop().then_inc(s_cw, 1)
            yg = 0
            for ei in range(NE):
                half = ei // 8
                e = ei % 8
                if e == 0 and half == 1:
                    v.wait_ge(s_ysp, 16)
                for t2 in range(2):
                    for tt4 in range(4):
                        gtt = half * 8 + t2 * 4 + tt4
                        ysl_t = t2 * 4 + tt4
                        for dh in range(2):
                            v.wait_ge(s_ymm, yg + 1)
                            ps = psYs[yg % 3]
                            ysl = y_sb[:, ysl_t, dh * 512:(dh + 1) * 512]
                            if e == 0:
                                v.tensor_scalar(ysl, ps[:], cw_sb[:, gtt, e:e + 1], None,
                                                ALU.mult).then_inc(s_yev, 1)
                            else:
                                v.scalar_tensor_tensor(ysl, ps[:], cw_sb[:, gtt, e:e + 1],
                                                       ysl, ALU.mult, ALU.add
                                                       ).then_inc(s_yev, 1)
                            yg += 1
            sg = 0
            for tt in range(16):
                par = tt % 2
                for dh in range(2):
                    v.wait_ge(s_smm3, sg + 1)
                    ps = psYs[sg % 3]
                    v.tensor_tensor(s_tmp[:, dh * 512:(dh + 1) * 512], ps[:],
                                    bout_sb[:, dh * 512:(dh + 1) * 512], ALU.add
                                    ).then_inc(s_sev3, 1)
                    sg += 1
                v.wait_ge(s_gs, tt + 1)
                v.wait_ge(s_xld2[par], (tt // 2 + 1) * 16)
                if tt >= 2:
                    v.wait_ge(s_outp[par], (tt // 2) * 16)
                v.drain()
                v.tensor_tensor(s_tmp[:], s_tmp[:], g_ep[:, par, :], ALU.mult)
                v.drain()
                v.tensor_tensor(out_sb[:, par, :], s_tmp[:], x_ep[:, par, :], ALU.add)
                v.drain()
                v.engine_nop().then_inc(s_ep, 1)
    es.close()
    return d, out_o


class Arena:
    def __init__(self, nc, nbytes):
        self.guard = nc.sbuf_tensor("arena", [128, nbytes], mybir.dt.uint8)
        self.t = self.guard.__enter__()
        self.off = 0
        self.nbytes = nbytes

    def reset(self):
        self.off = 0

    def alloc(self, shape, dtype):
        es = {F32: 4, BF16: 2, U32: 4}[dtype]
        free = 1
        for s in shape[1:]:
            free *= s
        nb = free * es
        off = (self.off + 63) // 64 * 64
        self.off = off + nb
        assert self.off <= self.nbytes, (shape, dtype, self.off)
        v = self.t[0:shape[0], off:off + nb].bitcast(dtype)
        if len(shape) > 2:
            names = " ".join(f"d{i}" for i in range(1, len(shape)))
            v = v.rearrange(f"p ({names}) -> p {names}",
                            **{f"d{i}": shape[i] for i in range(1, len(shape) - 1)})
        return v


def hostprep_fused(inp, core):
    m = hostprep_A(inp, core)
    cc = chunk_consts()
    NPg = NCH_TOT // 2
    tmtT = np.zeros((128, NPg, CH), np.float32)
    for pair in range(NPg):
        for par in range(2):
            i = 2 * pair + par
            tmtT[64 * par:64 * par + 64, pair, :] = cc['Tmat'][i].T
    wcol = np.zeros((128, NCH_TOT), np.float32)
    acol = np.zeros((128, NCH_TOT), np.float32)
    ccol = np.zeros((128, NCH_TOT), np.float32)
    for i in range(NCH_TOT):
        for par in range(2):
            wcol[64 * par:64 * par + 64, i] = cc['w'][i]
            acol[64 * par:64 * par + 64, i] = cc['a'][i]
            ccol[64 * par:64 * par + 64, i] = cc['c'][i]
    striu = np.zeros((128, CH), np.float32)
    stril = np.zeros((128, CH), np.float32)
    identt = np.zeros((128, 128), np.float32)
    ident2 = np.zeros((128, CH), np.float32)
    for p in range(128):
        c1 = p % 64
        striu[p, c1 + 1:] = 1.0
        stril[p, :c1] = 1.0
        identt[p, c1] = 1.0
        identt[p, 64 + c1] = 1.0
        ident2[p, c1] = 1.0
    bf = ml_dtypes.bfloat16
    x = np.asarray(inp['L0_attn_x'])[0]
    m.update({
        'tmtT_p': tmtT.astype(bf), 'wcol': wcol, 'acol': acol, 'ccol': ccol,
        'striu': striu.astype(bf), 'stril': stril.astype(bf),
        'ident2': ident2.astype(bf), 'identt': identt.astype(bf),
        'rmsw_b': np.tile(np.asarray(inp['L0_attn_rms_w'])[None, :], (128, 1)).astype(np.float32),
        'x_i': np.ascontiguousarray(x[core * TOK:(core + 1) * TOK]),
        'router_w': np.asarray(inp['L0_ffn_router_w']),
        'w1': np.asarray(inp['L0_ffn_w1']),
        'w2': np.asarray(inp['L0_ffn_w2']),
        'Wout': np.asarray(inp['L0_attn_Wout']),
        'bout_bc': np.tile(np.asarray(inp['L0_attn_bout'])[None, :], (128, 1)).astype(np.float32),
        'iota8': np.tile(np.arange(8, dtype=np.float32), (128, 1)),
    })
    return m


def build_fused(nc):
    import contextlib
    from concourse.ap import AP as _AP
    es = contextlib.ExitStack()
    C = es.enter_context
    d = {}
    d['x_ext'] = nc.dram_tensor("x_ext", [EXT, DM], F32, kind="ExternalInput").ap()
    d['halo_mask'] = nc.dram_tensor("halo_mask", [128, 1], F32, kind="ExternalInput").ap()
    for nm in ("Wk", "Wq", "Wv"):
        d[nm] = nc.dram_tensor(nm, [DM, DM], F32, kind="ExternalInput").ap()
    d['Wbeta'] = nc.dram_tensor("Wbeta", [H, DM], F32, kind="ExternalInput").ap()
    for nm in ("bk_t", "bq_t", "bv_t"):
        d[nm] = nc.dram_tensor(nm, [128, 8], F32, kind="ExternalInput").ap()
    d['bbeta_t'] = nc.dram_tensor("bbeta_t", [16, 1], F32, kind="ExternalInput").ap()
    for nm in ("ck", "cq", "cv"):
        d[nm] = nc.dram_tensor(nm, [128, 8, 4], F32, kind="ExternalInput").ap()
    d['ind16'] = nc.dram_tensor("ind16", [8, 128, 16], BF16, kind="ExternalInput").ap()
    d['ind16T'] = nc.dram_tensor("ind16T", [8, 16, 128], BF16, kind="ExternalInput").ap()
    NPg = NCH_TOT // 2
    d['tmtT_p'] = nc.dram_tensor("tmtT_p", [128, NPg, CH], BF16, kind="ExternalInput").ap()
    for nm in ("wcol", "acol", "ccol"):
        d[nm] = nc.dram_tensor(nm, [128, NCH_TOT], F32, kind="ExternalInput").ap()
    for nm in ("striu", "stril", "ident2"):
        d[nm] = nc.dram_tensor(nm, [128, CH], BF16, kind="ExternalInput").ap()
    d['identt'] = nc.dram_tensor("identt", [128, 128], BF16, kind="ExternalInput").ap()
    d['rmsw_b'] = nc.dram_tensor("rmsw_b", [128, CH], F32, kind="ExternalInput").ap()
    d['x_i'] = nc.dram_tensor("x_i", [TOK, DM], F32, kind="ExternalInput").ap()
    d['router_w'] = nc.dram_tensor("router_w", [DM, E], F32, kind="ExternalInput").ap()
    d['w1'] = nc.dram_tensor("w1", [E, DM, FF], F32, kind="ExternalInput").ap()
    d['w2'] = nc.dram_tensor("w2", [E, FF, DM], F32, kind="ExternalInput").ap()
    d['Wout'] = nc.dram_tensor("Wout", [DM, DM], F32, kind="ExternalInput").ap()
    d['bout_bc'] = nc.dram_tensor("bout_bc", [128, DM], F32, kind="ExternalInput").ap()
    d['iota8'] = nc.dram_tensor("iota8", [128, E], F32, kind="ExternalInput").ap()
    out_o = nc.dram_tensor("out_o", [TOK, DM], F32, kind="ExternalOutput").ap()
    wbf = nc.dram_tensor("wbf", [DM, DM], BF16, kind="Internal").ap()
    wobf = nc.dram_tensor("wobf", [DM, DM], BF16, kind="Internal").ap()
    yhbm = nc.dram_tensor("yhbm", [2, 128, 8, DM], F32, kind="Internal").ap()
    cc_in = nc.dram_tensor("cc_in", [4, 128, 8, TOK], BF16, kind="Internal").ap()
    cc_out = nc.dram_tensor("cc_out", [4, 8, 128, 8, TOK], BF16, kind="Internal").ap()
    xn_stage = nc.dram_tensor("xn_stage", [128, 8, TOK], BF16, kind="Internal").ap()
    o_stage = nc.dram_tensor("o_stage", [128, 128, 128], BF16, kind="Internal").ap()
    o_all = nc.dram_tensor("o_all", [8, 128, 128, 128], BF16, kind="Internal").ap()
    o_mine = nc.dram_tensor("o_mine", [8, 128, 16, 128], BF16, kind="Internal").ap()

    AR = Arena(nc, 212000)
    banks = [C(nc.psum_tensor(f"bank{i}", [128, 512], F32)) for i in range(8)]
    rg = [list(range(8))]
    POOL = mybir.EngineType.Pool
    ACTE = mybir.EngineType.Activation
    s_cc1 = C(nc.semaphore("s_cc1"))
    s_cc2 = C(nc.semaphore("s_cc2"))
    block = C(nc.Block())

    fxA = dict(d=d, outsA={'kh_o': cc_in[0], 'bt_o': cc_in[1], 'qh_o': cc_in[2],
                           'vt_o': cc_in[3], 'xn_o': xn_stage[:]},
               wbf=wbf, alloc=AR.alloc, banks=banks, block=block)

    def cc_after(g, ti):
        g.wait_ge(fxA['s_outA'], 16 * (ti + 1))
        g.collective_compute("AllGather", ALU.bypass, rg,
                             ins=[cc_in[ti]], outs=[cc_out[ti]]
                             ).then_inc(s_cc1, 1)
    fxA['cc_after'] = cc_after
    with nc.named_scope("phaseA"):
        build_phaseA(nc, fx=fxA)

    def _cc1(g):
        g.wait_ge(fxA['s_outA'], 80)
    with nc.named_scope("cc1"):
        block.gpsimd(_cc1)
    nc.all_engine_barrier()
    AR.reset()

    def b_loads(g, bufs, s_bld):
        pid = nc.partition_id(engines=[POOL])
        for ti, buf in enumerate((bufs['kh'], bufs['btl'], bufs['qh'], bufs['vt'])):
            g.wait_ge(s_cc1, ti + 1)
            tmpl = cc_out[ti, :, :, 0, :].transpose([1, 0, 2])
            ap = _AP(tmpl.tensor, tmpl.offset + pid * TOK, tmpl.ap)
            g.dma_start(out=buf[:].rearrange("p (r t) -> p r t", r=8), in_=ap
                        ).then_inc(s_bld, 16)

    fxB = dict(d=d, o_o=o_stage[:], b_loads=b_loads,
               alloc=AR.alloc, banks=banks, block=block)
    with nc.named_scope("phaseB"):
        build_phaseB(nc, fx=fxB)

    def _cc2(g):
        g.collective_compute("AllGather", ALU.bypass, rg,
                             ins=[o_stage[:]], outs=[o_all[:]]).then_inc(s_cc2, 1)
    block.gpsimd(_cc2)
    nc.all_engine_barrier()
    AR.reset()

    pid_cell = [None]

    def o_src(tc, fc):
        return o_mine[fc, :, tc, :]

    def o_gather(g):
        g.wait_ge(s_cc2, 1)
        pid = nc.partition_id(engines=[POOL])
        tmpl = o_all[:, :, 0:16, :]
        ap = _AP(tmpl.tensor, tmpl.offset + pid * (16 * 128), tmpl.ap)
        g.dma_start(out=o_mine[:], in_=ap).then_inc(s_cc2, 16)

    fxC = dict(d=d, out_o=out_o, wobf=wobf, yhbm=yhbm,
               xn_src=xn_stage[:], o_src=o_src,
               o_gather=o_gather, s_cc2=s_cc2,
               alloc=AR.alloc, banks=banks, block=block)
    with nc.named_scope("phaseC"):
        build_phaseC(nc, fx=fxC)
    es.close()
    return d, out_o

from concourse.bass_utils import run_bass_kernel_spmd


def kernel(**inputs):
    inp = {k: np.asarray(v) for k, v in inputs.items()}
    nc = mk_nc()
    build_fused(nc)
    res = run_bass_kernel_spmd(nc, [hostprep_fused(inp, c) for c in range(NCORES)],
                               core_ids=list(range(NCORES)))
    out = np.concatenate([np.asarray(res.results[c]['out_o'], dtype=np.float32)
                          for c in range(NCORES)], axis=0)
    return out[None]
